# version 01
# baseline (speedup 1.0000x reference)
import numpy as np
import concourse.bass as bass
import concourse.mybir as mybir
from concourse.bass_utils import run_bass_kernel_spmd

F32 = mybir.dt.float32
BF16 = mybir.dt.bfloat16
I32 = mybir.dt.int32
U8 = mybir.dt.uint8
AF = mybir.ActivationFunctionType
ALU = mybir.AluOpType
AX = mybir.AxisListType

D_MODEL = 1024
D_INNER = 2048
D_STATE = 16
D_CONV = 4
DT_RANK = 64
BATCH = 4
SEQ = 4096
TPC = 2048
N_CORES = 8


NDMASLOT = 12


class _Q:
    def __init__(self, name, eng, sem_c, sem_d_list):
        self.name = name
        self.eng = eng
        self.semc = sem_c
        self.semd = sem_d_list
        self.ccount = 0
        self.dcount = 0
        self.waited = {}


class Emitter:

    def __init__(self, nc, ctx):
        self.nc = nc
        self.writers = {}
        self.readers = {}
        self.qs = {}
        for name, eng in [("pe", nc.tensor), ("act", nc.scalar), ("dve", nc.vector),
                          ("pool", nc.gpsimd), ("sp", nc.sync)]:
            sc = ctx.enter_context(nc.semaphore(f"semc_{name}"))
            sds = [ctx.enter_context(nc.semaphore(f"semd_{name}{i}")) for i in range(NDMASLOT)]
            self.qs[name] = _Q(name, eng, sc, sds)

    def _wait_spec(self, q, spec, force_same=False):
        if spec[0] == "c":
            _, pq_name, count = spec
            key = (pq_name, "c")
            if q.waited.get(key, -1) >= count:
                return
            q.eng.wait_ge(self.qs[pq_name].semc, count)
            q.waited[key] = count
        else:
            _, pq_name, j = spec
            slot = j % NDMASLOT
            count = 16 * (j // NDMASLOT + 1)
            key = (pq_name, "d", slot)
            if q.waited.get(key, -1) >= count:
                return
            q.eng.wait_ge(self.qs[pq_name].semd[slot], count)
            q.waited[key] = count

    def op(self, qname, fn, reads=(), writes=(), scalar_reads=(), dma=False):
        q = self.qs[qname]
        if getattr(self, "_fence", None) is not None and self._fence_seen.get(qname) is not self._fence:
            for spec in self._fence:
                self._wait_spec(q, spec)
            self._fence_seen[qname] = self._fence
        for k in reads:
            w = self.writers.get(k)
            if w is not None:
                self._wait_spec(q, w)
        for k in scalar_reads:
            w = self.writers.get(k)
            if w is not None:
                self._wait_spec(q, w, force_same=True)
        for k in writes:
            w = self.writers.get(k)
            if w is not None:
                self._wait_spec(q, w)
            for rtok, rspec in self.readers.get(k, {}).items():
                self._wait_spec(q, rspec)
        if dma:
            j = q.dcount
            if j >= NDMASLOT:
                self._wait_spec(q, ("d", qname, j - NDMASLOT))
            inst = fn(q.eng)
            inst.then_inc(q.semd[j % NDMASLOT], 16)
            q.dcount += 1
            tok = ("d", qname, j)
        else:
            inst = fn(q.eng)
            inst.then_inc(q.semc, 1)
            q.ccount += 1
            tok = ("c", qname, q.ccount)
        for k in list(reads) + list(scalar_reads):
            self.readers.setdefault(k, {})[tok if tok[0] == "d" else tok[:2]] = tok
        for k in writes:
            self.writers[k] = tok
            self.readers[k] = {}
        return inst

    def fence(self):
        specs = []
        for pq in self.qs.values():
            if pq.ccount > 0:
                specs.append(("c", pq.name, pq.ccount))
            for j in range(max(0, pq.dcount - NDMASLOT), pq.dcount):
                specs.append(("d", pq.name, j))
        self._fence = tuple(specs)
        self._fence_seen = {}

    def drain(self):
        q = self.qs["pool"]
        for pq in self.qs.values():
            if pq.ccount > 0 and pq.name != q.name:
                self._wait_spec(q, ("c", pq.name, pq.ccount))
            for j in range(max(0, pq.dcount - NDMASLOT), pq.dcount):
                self._wait_spec(q, ("d", pq.name, j))


def build_a1():
    nc = bass.Bass("TRN2", target_bir_lowering=False, debug=False)
    TH = TPC + 3
    xT = nc.dram_tensor("xT", [D_MODEL, TH], F32, kind="ExternalInput").ap()
    W_inT = nc.dram_tensor("W_inT", [D_MODEL, 2 * D_INNER], F32, kind="ExternalInput").ap()
    conv_w = nc.dram_tensor("conv_w", [128, 16 * D_CONV], F32, kind="ExternalInput").ap()
    conv_b = nc.dram_tensor("conv_b", [128, 16], F32, kind="ExternalInput").ap()
    W_xT = nc.dram_tensor("W_xT", [D_INNER, 96], F32, kind="ExternalInput").ap()
    W_dtT = nc.dram_tensor("W_dtT", [DT_RANK, D_INNER], F32, kind="ExternalInput").ap()
    b_dt = nc.dram_tensor("b_dt", [128, 16], F32, kind="ExternalInput").ap()

    o_xconv = nc.dram_tensor("xconv", [D_INNER, TPC], F32, kind="ExternalOutput").ap()
    o_dt = nc.dram_tensor("dt", [D_INNER, TPC], F32, kind="ExternalOutput").ap()
    o_dtx = nc.dram_tensor("dtx", [D_INNER, TPC], BF16, kind="ExternalOutput").ap()
    o_zs = nc.dram_tensor("zs", [D_INNER, TPC], BF16, kind="ExternalOutput").ap()
    o_bc = nc.dram_tensor("bc", [32, TPC], F32, kind="ExternalOutput").ap()

    from contextlib import ExitStack
    ctx = ExitStack()
    sb = lambda name, shape, dt: ctx.enter_context(nc.sbuf_tensor(name, shape, dt))
    ps = lambda name, shape, dt=F32: ctx.enter_context(nc.psum_tensor(name, shape, dt))

    stage = sb("stage", [128, 4096], F32)
    winb = [sb(f"winb{k}", [128, 2 * D_INNER], BF16) for k in range(8)]
    xtb = [sb(f"xtb{k}", [128, TH], BF16) for k in range(8)]
    wxb = sb("wxb", [128, 16 * 96], BF16)
    wdtb = sb("wdtb", [64, D_INNER], BF16)
    cwt = sb("cwt", [128, 16 * D_CONV], F32)
    cbt = sb("cbt", [128, 16], F32)
    bdt_t = sb("bdt", [128, 16], F32)
    xin = sb("xin", [128, TH], F32)
    xcv = sb("xcv", [128, TPC], F32)
    xcb = sb("xcb", [128, TPC], BF16)
    xdbl = sb("xdbl", [96, TPC], F32)
    dtrb = sb("dtrb", [64, TPC], BF16)
    dtt = sb("dtt", [128, TPC], F32)
    dtxt = sb("dtxt", [128, TPC], BF16)
    sp1 = sb("sp1", [128, 512], F32)
    sp2 = sb("sp2", [128, 512], F32)
    zst = sb("zst", [128, TPC], BF16)

    pband = [ps(f"pband{i}", [128, 512]) for i in range(4)]
    pxdbl = [ps(f"pxdbl{i}", [96, 512]) for i in range(4)]

    with ctx, nc.Block() as block:
        em = Emitter(nc, ctx)

        def everything(_):
            def eload(dst, src, key, qn="sp"):
                em.op(qn, lambda e: e.dma_start(out=dst, in_=src), writes=[key], dma=True)

            eload(cwt[:], conv_w, "cwt")
            eload(cbt[:], conv_b, "cbt")
            eload(bdt_t[:], b_dt, "bdt")
            for k in range(8):
                em.op("sp", lambda e, k=k: e.dma_start(out=stage[:], in_=W_inT[k * 128:(k + 1) * 128, :]),
                      reads=[], writes=["stage"], dma=True)
                em.op("dve", lambda e, k=k: e.tensor_copy(winb[k][:], stage[:]),
                      reads=["stage"], writes=[f"winb{k}"])
            for k in range(8):
                em.op("sp", lambda e, k=k: e.dma_start(out=stage[:, :TH], in_=xT[k * 128:(k + 1) * 128, :]),
                      writes=["stage"], dma=True)
                em.op("dve", lambda e, k=k: e.tensor_copy(xtb[k][:], stage[:, :TH]),
                      reads=["stage"], writes=[f"xtb{k}"])
            for c in range(16):
                em.op("sp", lambda e, c=c: e.dma_start(out=stage[:, :96], in_=W_xT[c * 128:(c + 1) * 128, :]),
                      writes=["stage"], dma=True)
                em.op("dve", lambda e, c=c: e.tensor_copy(wxb[:, c * 96:(c + 1) * 96], stage[:, :96]),
                      reads=["stage"], writes=["wxb"])
            em.op("sp", lambda e: e.dma_start(out=stage[:64, :D_INNER], in_=W_dtT),
                  writes=["stage"], dma=True)
            em.op("dve", lambda e: e.tensor_copy(wdtb[:], stage[:64, :D_INNER]),
                  reads=["stage"], writes=["wdtb"])

            NG = [(0, 512), (512, 512), (1024, 512), (1536, 512)]
            for m in range(32):
                inner = m < 16
                groups = NG + [(2048, 3)] if inner else [(3 + g0, gl) for g0, gl in NG]
                for gi, (g0, gl) in enumerate(groups):
                    pt = pband[gi % 4]
                    pkey = f"pband{gi % 4}"
                    for k in range(8):
                        em.op("pe", lambda e, m=m, k=k, g0=g0, gl=gl, pt=pt: e.matmul(
                            pt[:, :gl], winb[k][:, m * 128:(m + 1) * 128], xtb[k][:, g0:g0 + gl],
                            start=(k == 0), stop=(k == 7)),
                            reads=[f"winb{k}", f"xtb{k}"],
                            writes=[pkey] if k in (0, 7) else [])
                    if inner:
                        em.op("act", lambda e, g0=g0, gl=gl, pt=pt: e.copy(xin[:, g0:g0 + gl], pt[:, :gl]),
                              reads=[pkey], writes=["xin"])
                if inner:
                    em.op("dve", lambda e, m=m: e.tensor_scalar(
                        xcv[:], xin[:, 0:TPC], cwt[:, m * 4 + 0: m * 4 + 1], None, ALU.mult),
                        reads=["xin"], scalar_reads=["cwt"], writes=["xcv"])
                    for k in range(1, 4):
                        em.op("dve", lambda e, m=m, k=k: e.scalar_tensor_tensor(
                            xcv[:], xin[:, k:k + TPC], cwt[:, m * 4 + k: m * 4 + k + 1], xcv[:],
                            ALU.mult, ALU.add),
                            reads=["xin", "xcv"], scalar_reads=["cwt"], writes=["xcv"])
                    em.op("act", lambda e, m=m: e.activation(
                        xcv[:], xcv[:], AF.Silu, bias=cbt[:, m:m + 1]),
                        reads=["xcv"], scalar_reads=["cbt"], writes=["xcv"])
                    em.op("dve", lambda e: e.tensor_copy(xcb[:], xcv[:]),
                          reads=["xcv"], writes=["xcb"])
                    em.op("sp", lambda e, m=m: e.dma_start(out=o_xconv[m * 128:(m + 1) * 128, :], in_=xcv[:]),
                          reads=["xcv"], dma=True)
                    for gi in range(4):
                        em.op("pe", lambda e, m=m, gi=gi: e.matmul(
                            pxdbl[gi][:], wxb[:, m * 96:(m + 1) * 96], xcb[:, gi * 512:(gi + 1) * 512],
                            start=(m == 0), stop=(m == 15)),
                            reads=["wxb", "xcb"],
                            writes=[f"pxdbl{gi}"] if m in (0, 15) else [])
                if not inner:
                    for gi, (g0, gl) in enumerate(groups):
                        pt = pband[gi % 4]
                        pkey = f"pband{gi % 4}"
                        em.op("act", lambda e, gi=gi, pt=pt: e.activation(
                            zst[:, gi * 512:(gi + 1) * 512], pt[:], AF.Silu),
                            reads=[pkey], writes=["zst"])
                    em.op("sp", lambda e, m=m: e.dma_start(
                        out=o_zs[(m - 16) * 128:(m - 15) * 128, :], in_=zst[:]),
                        reads=["zst"], dma=True)

            for gi in range(4):
                em.op("act", lambda e, gi=gi: e.copy(xdbl[:, gi * 512:(gi + 1) * 512], pxdbl[gi][:]),
                      reads=[f"pxdbl{gi}"], writes=["xdbl"])
            em.op("dve", lambda e: e.tensor_copy(dtrb[:], xdbl[0:64, :]),
                  reads=["xdbl"], writes=["dtrb"])
            em.op("sp", lambda e: e.dma_start(out=o_bc, in_=xdbl[64:96, :]),
                  reads=["xdbl"], dma=True)

            for m in range(16):
                for gi in range(4):
                    em.op("pe", lambda e, m=m, gi=gi: e.matmul(
                        pband[gi][:], wdtb[:, m * 128:(m + 1) * 128], dtrb[:, gi * 512:(gi + 1) * 512],
                        start=True, stop=True),
                        reads=["wdtb", "dtrb"], writes=[f"pband{gi}"])
                for gi in range(4):
                    sl = slice(gi * 512, (gi + 1) * 512)
                    em.op("act", lambda e, m=m, gi=gi, sl=sl: e.activation(
                        sp1[:], pband[gi][:], AF.Abs, bias=bdt_t[:, m:m + 1]),
                        reads=[f"pband{gi}"], scalar_reads=["bdt"], writes=["sp1"])
                    em.op("act", lambda e: e.activation(sp2[:], sp1[:], AF.Exp, scale=-1.0),
                          reads=["sp1"], writes=["sp2"])
                    em.op("act", lambda e: e.activation(sp1[:], sp2[:], AF.Ln, bias=1.0),
                          reads=["sp2"], writes=["sp1"])
                    em.op("act", lambda e, m=m, gi=gi: e.activation(
                        sp2[:], pband[gi][:], AF.Relu, bias=bdt_t[:, m:m + 1]),
                        reads=[f"pband{gi}"], scalar_reads=["bdt"], writes=["sp2"])
                    em.op("dve", lambda e, sl=sl: e.tensor_add(dtt[:, sl], sp1[:], sp2[:]),
                          reads=["sp1", "sp2"], writes=["dtt"])
                em.op("sp", lambda e, m=m: e.dma_start(out=o_dt[m * 128:(m + 1) * 128, :], in_=dtt[:]),
                      reads=["dtt"], dma=True)
                em.op("pool", lambda e, m=m: e.dma_start(out=xcv[:], in_=o_xconv[m * 128:(m + 1) * 128, :]),
                      writes=["xcv"], dma=True)
                em.op("dve", lambda e: e.tensor_mul(dtxt[:], dtt[:], xcv[:]),
                      reads=["dtt", "xcv"], writes=["dtxt"])
                em.op("pool", lambda e, m=m: e.dma_start(out=o_dtx[m * 128:(m + 1) * 128, :], in_=dtxt[:]),
                      reads=["dtxt"], dma=True)

            em.drain()

        everything(None)

    return nc


def _host_shards_a1(inputs):
    x = np.asarray(inputs["L0_attn_x"])
    W_in = np.asarray(inputs["L0_attn_W_in"])
    conv_w = np.asarray(inputs["L0_attn_conv_w"]).reshape(D_INNER, D_CONV)
    conv_b = np.asarray(inputs["L0_attn_conv_b"])
    W_x = np.asarray(inputs["L0_attn_W_x"])
    W_dt = np.asarray(inputs["L0_attn_W_dt"])
    b_dt = np.asarray(inputs["L0_attn_b_dt"])

    W_inT = np.ascontiguousarray(W_in.T)
    W_xT = np.ascontiguousarray(W_x.T)
    W_dtT = np.ascontiguousarray(W_dt.T)
    cw = np.ascontiguousarray(conv_w.reshape(16, 128, D_CONV).transpose(1, 0, 2).reshape(128, 16 * D_CONV))
    cb = np.ascontiguousarray(conv_b.reshape(16, 128).T)
    bd = np.ascontiguousarray(b_dt.reshape(16, 128).T)

    in_maps = []
    for core in range(N_CORES):
        b, h = divmod(core, 2)
        t0 = h * TPC
        xs = np.zeros((D_MODEL, TPC + 3), np.float32)
        sl = x[b, max(0, t0 - 3):t0 + TPC, :]
        xs[:, (TPC + 3) - sl.shape[0]:] = sl.T
        in_maps.append({
            "xT": np.ascontiguousarray(xs),
            "W_inT": W_inT, "conv_w": cw, "conv_b": cb,
            "W_xT": W_xT, "W_dtT": W_dtT, "b_dt": bd,
        })
    return in_maps


def build_a2():
    nc = bass.Bass("TRN2", target_bir_lowering=False, debug=False)
    TC = 512
    NTC = SEQ // TC
    dtp = nc.dram_tensor("dtp", [256, BATCH, SEQ], F32, kind="ExternalInput").ap()
    dtxp = nc.dram_tensor("dtxp", [256, BATCH, SEQ], BF16, kind="ExternalInput").ap()
    bcp = nc.dram_tensor("bcp", [BATCH, 32, SEQ], F32, kind="ExternalInput").ap()
    alog = nc.dram_tensor("alog", [128, 2 * D_STATE], F32, kind="ExternalInput").ap()
    o_y = nc.dram_tensor("y", [256, BATCH, SEQ], F32, kind="ExternalOutput").ap()

    from contextlib import ExitStack
    ctx = ExitStack()
    sb = lambda name, shape, dt: ctx.enter_context(nc.sbuf_tensor(name, shape, dt))
    ps = lambda name, shape, dt=F32: ctx.enter_context(nc.psum_tensor(name, shape, dt))

    at = sb("at", [128, 2 * D_STATE], F32)
    sel = sb("sel", [32, 32 * 128], F32)
    bct = sb("bct", [32, TC], F32)
    Bb = sb("Bb", [128, TC, D_STATE], BF16)
    Cb = sb("Cb", [128, TC, D_STATE], BF16)
    dtt = [sb(f"dtt{i}", [128, TC], F32) for i in range(2)]
    dtxt = [sb(f"dtxt{i}", [128, TC], BF16) for i in range(2)]
    dA = sb("dA", [128, TC, D_STATE], F32)
    ut = sb("ut", [128, TC, D_STATE], BF16)
    st = [sb(f"st{d}", [128, TC, D_STATE], F32) for d in range(2)]
    yt = sb("yt", [128, TC], F32)
    pbc = [ps(f"pbc{i}", [128, TC]) for i in range(2)]

    with ctx, nc.Block() as block:
        em = Emitter(nc, ctx)
        em.op("sp", lambda e: e.dma_start(out=at[:], in_=alog), writes=["at"], dma=True)
        em.op("act", lambda e: e.activation(at[:], at[:], AF.Exp), reads=["at"], writes=["at"])
        em.op("dve", lambda e: e.tensor_scalar_mul(at[:], at[:], -1.0), reads=["at"], writes=["at"])
        em.op("pool", lambda e: e.memset(sel[:], 1.0), writes=["sel"])
        for n in range(32):
            em.op("pool", lambda e, n=n: e.affine_select(
                out=sel[:, n * 128:(n + 1) * 128], in_=sel[:, n * 128:(n + 1) * 128],
                compare_op=ALU.is_equal, fill=0.0, base=-n, pattern=[[0, 128]],
                channel_multiplier=1), reads=["sel"], writes=["sel"] if n == 31 else [])

        for b in range(BATCH):
            for tc in range(NTC):
                t0 = tc * TC
                em.op("sp", lambda e, b=b, t0=t0: e.dma_start(out=bct[:], in_=bcp[b, :, t0:t0 + TC]),
                      writes=["bct"], dma=True)
                for n in range(32):
                    dst = (Bb if n < 16 else Cb)
                    dkey = "Bb" if n < 16 else "Cb"
                    pi = n % 2
                    em.op("pe", lambda e, n=n, pi=pi: e.matmul(
                        pbc[pi][:], sel[:, n * 128:(n + 1) * 128], bct[:], start=True, stop=True),
                        reads=["sel", "bct"], writes=[f"pbc{pi}"])
                    em.op("act", lambda e, n=n, pi=pi, dst=dst: e.copy(
                        dst[:, :, n % 16], pbc[pi][:]),
                        reads=[f"pbc{pi}"], writes=[dkey])
                for dc in range(2):
                    db = tc % 2
                    em.op("sp", lambda e, b=b, t0=t0, dc=dc, db=db: e.dma_start(
                        out=dtt[db][:], in_=dtp[dc * 128:(dc + 1) * 128, b, t0:t0 + TC]),
                        writes=[f"dtt{db}"], dma=True)
                    em.op("pool", lambda e, b=b, t0=t0, dc=dc, db=db: e.dma_start(
                        out=dtxt[db][:], in_=dtxp[dc * 128:(dc + 1) * 128, b, t0:t0 + TC]),
                        writes=[f"dtxt{db}"], dma=True)
                    for n in range(D_STATE):
                        em.op("act", lambda e, n=n, dc=dc, db=db: e.activation(
                            dA[:, :, n], dtt[db][:], AF.Exp, scale=at[:, dc * 16 + n:dc * 16 + n + 1]),
                            reads=[f"dtt{db}"], scalar_reads=["at"], writes=["dA"] if n in (0, 15) else [])
                    em.op("dve", lambda e, db=db: e.tensor_tensor(
                        ut[:], dtxt[db][:, :, None].broadcast_to([128, TC, D_STATE]), Bb[:], ALU.mult),
                        reads=[f"dtxt{db}", "Bb"], writes=["ut"])
                    cur = st[dc]
                    ckey = f"st{dc}"
                    for n in range(D_STATE):
                        init = 0.0 if tc == 0 else cur[:, TC - 1, n:n + 1]
                        em.op("dve", lambda e, n=n, cur=cur, init=init: e.tensor_tensor_scan(
                            cur[:, :, n], dA[:, :, n], ut[:, :, n], init, ALU.mult, ALU.add),
                            reads=["dA", "ut"], scalar_reads=[ckey] if tc > 0 else [],
                            writes=[ckey] if n in (0, 15) else [])
                    em.op("dve", lambda e, cur=cur: e.tensor_tensor(ut[:], cur[:], Cb[:], ALU.mult),
                          reads=[ckey, "Cb"], writes=["ut"])
                    em.op("dve", lambda e: e.tensor_reduce(yt[:, :, None], ut[:], AX.X, ALU.add),
                          reads=["ut"], writes=["yt"])
                    em.op("sp", lambda e, b=b, t0=t0, dc=dc: e.dma_start(
                        out=o_y[dc * 128:(dc + 1) * 128, b, t0:t0 + TC], in_=yt[:]),
                        reads=["yt"], dma=True)
        em.drain()
    return nc


def build_a3():
    nc = bass.Bass("TRN2", target_bir_lowering=False, debug=False)
    yT = nc.dram_tensor("yT", [D_INNER, TPC], F32, kind="ExternalInput").ap()
    zsT = nc.dram_tensor("zsT", [D_INNER, TPC], BF16, kind="ExternalInput").ap()
    xcT = nc.dram_tensor("xcT", [D_INNER, TPC], F32, kind="ExternalInput").ap()
    Dv = nc.dram_tensor("Dv", [128, 16], F32, kind="ExternalInput").ap()
    WoT = nc.dram_tensor("WoT", [D_INNER, D_MODEL], F32, kind="ExternalInput").ap()
    xresT = nc.dram_tensor("xresT", [D_MODEL, TPC], F32, kind="ExternalInput").ap()
    o_x1 = nc.dram_tensor("x1", [D_MODEL, TPC], F32, kind="ExternalOutput").ap()

    from contextlib import ExitStack
    ctx = ExitStack()
    sb = lambda name, shape, dt: ctx.enter_context(nc.sbuf_tensor(name, shape, dt))

    stage = sb("stage", [128, 1024], F32)
    yt = sb("yt", [128, 512], F32)
    zt = sb("zt", [128, 512], BF16)
    xct = sb("xct", [128, 512], F32)
    yfk = sb("yfk", [128, 512], BF16)
    dvt = sb("dvt", [128, 16], F32)
    wob = [sb(f"wob{k}", [128, D_MODEL], BF16) for k in range(16)]
    xsum = [sb(f"xsum{m}", [128, TPC], F32) for m in range(8)]
    sqt = sb("sqt", [128, 512], F32)
    statv = sb("statv", [1, 2 * TPC], F32)
    rm = sb("rm", [1, 2 * TPC], F32)
    ones1 = sb("ones1", [1, 128], F32)
    onesc = sb("onesc", [128, 1], F32)
    st2 = sb("st2", [128, 512], F32)
    epst = sb("epst", [1, 1], F32)

    with ctx, nc.Block() as block:
        em = Emitter(nc, ctx)
        em.op("pool", lambda e: e.memset(ones1[:], 1.0), writes=["ones1"])
        em.op("pool", lambda e: e.memset(onesc[:], 1.0), writes=["onesc"])
        em.op("pool", lambda e: e.memset(epst[:], 1e-5), writes=["epst"])
        em.op("sp", lambda e: e.dma_start(out=dvt[:], in_=Dv), writes=["dvt"], dma=True)
        for k in range(16):
            em.op("sp", lambda e, k=k: e.dma_start(out=stage[:], in_=WoT[k * 128:(k + 1) * 128, :]),
                  writes=["stage"], dma=True)
            em.op("dve", lambda e, k=k: e.tensor_copy(wob[k][:], stage[:]),
                  reads=["stage"], writes=[f"wob{k}"])

        with ExitStack() as pctx:
            pdat = [pctx.enter_context(nc.psum_tensor(f"pdat{m}", [128, 512], F32)) for m in range(8)]
            for g in range(4):
                c0 = g * 512
                for k in range(16):
                    em.op("sp", lambda e, k=k, c0=c0: e.dma_start(out=yt[:], in_=yT[k * 128:(k + 1) * 128, c0:c0 + 512]),
                          writes=["yt"], dma=True)
                    em.op("pool", lambda e, k=k, c0=c0: e.dma_start(out=zt[:], in_=zsT[k * 128:(k + 1) * 128, c0:c0 + 512]),
                          writes=["zt"], dma=True)
                    em.op("pool", lambda e, k=k, c0=c0: e.dma_start(out=xct[:], in_=xcT[k * 128:(k + 1) * 128, c0:c0 + 512]),
                          writes=["xct"], dma=True)
                    em.op("dve", lambda e: e.tensor_mul(yt[:], yt[:], zt[:]),
                          reads=["yt", "zt"], writes=["yt"])
                    em.op("dve", lambda e, k=k: e.scalar_tensor_tensor(
                        yt[:], xct[:], dvt[:, k:k + 1], yt[:], ALU.mult, ALU.add),
                        reads=["yt", "xct"], scalar_reads=["dvt"], writes=["yt"])
                    em.op("dve", lambda e: e.tensor_copy(yfk[:], yt[:]),
                          reads=["yt"], writes=["yfk"])
                    for mo in range(8):
                        em.op("pe", lambda e, k=k, mo=mo: e.matmul(
                            pdat[mo][:], wob[k][:, mo * 128:(mo + 1) * 128], yfk[:],
                            start=(k == 0), stop=(k == 15)),
                            reads=[f"wob{k}", "yfk"], writes=[f"pdat{mo}"] if k in (0, 15) else [])
                for mo in range(8):
                    em.op("sp", lambda e, mo=mo, c0=c0: e.dma_start(
                        out=stage[:, 0:512], in_=xresT[mo * 128:(mo + 1) * 128, c0:c0 + 512]),
                        writes=["stage"], dma=True)
                    em.op("dve", lambda e, mo=mo, c0=c0: e.tensor_add(
                        xsum[mo][:, c0:c0 + 512], pdat[mo][:], stage[:, 0:512]),
                        reads=[f"pdat{mo}", "stage"], writes=[(f"xsum{mo}", c0)])

        em.fence()
        with ExitStack() as pctx:
            pst = pctx.enter_context(nc.psum_tensor("pst", [1, 1024], F32))
            pbc = pctx.enter_context(nc.psum_tensor("pbc", [128, 1024], F32))
            for g in range(4):
                c0 = g * 512
                for mo in range(8):
                    em.op("act", lambda e, mo=mo, c0=c0: e.activation(
                        sqt[:], xsum[mo][:, c0:c0 + 512], AF.Square),
                        reads=[(f"xsum{mo}", c0)], writes=["sqt"])
                    em.op("pe", lambda e, mo=mo, c0=c0: e.matmul(
                        pst[:, 0:512], onesc[:], xsum[mo][:, c0:c0 + 512],
                        start=(mo == 0), stop=(mo == 7)),
                        reads=[(f"xsum{mo}", c0), "onesc"], writes=["pst"] if mo in (0, 7) else [])
                    em.op("pe", lambda e, mo=mo: e.matmul(
                        pst[:, 512:1024], onesc[:], sqt[:],
                        start=(mo == 0), stop=(mo == 7)),
                        reads=["sqt", "onesc"], writes=["pst2"] if mo in (0, 7) else [])
                em.op("act", lambda e, c0=c0: e.copy(statv[:, c0:c0 + 512], pst[:, 0:512]),
                      reads=["pst"], writes=["statv"])
                em.op("act", lambda e, c0=c0: e.copy(statv[:, TPC + c0:TPC + c0 + 512], pst[:, 512:1024]),
                      reads=["pst2"], writes=["statv2"])
            em.op("dve", lambda e: e.tensor_scalar_mul(statv[:, 0:TPC], statv[:, 0:TPC], 1.0 / D_MODEL),
                  reads=["statv"], writes=["statv"])
            em.op("dve", lambda e: e.tensor_scalar_mul(statv[:, TPC:], statv[:, TPC:], 1.0 / D_MODEL),
                  reads=["statv2"], writes=["statv2"])
            em.op("dve", lambda e: e.scalar_tensor_tensor(
                rm[:, 0:TPC], statv[:, 0:TPC], -1.0, statv[:, 0:TPC], ALU.mult, ALU.mult),
                reads=["statv"], writes=["rm"])
            em.op("dve", lambda e: e.tensor_add(rm[:, 0:TPC], rm[:, 0:TPC], statv[:, TPC:]),
                  reads=["rm", "statv2"], writes=["rm"])
            em.op("act", lambda e: e.activation(rm[:, 0:TPC], rm[:, 0:TPC], AF.Ln, bias=epst[0:1, 0:1]),
                  reads=["rm"], scalar_reads=["epst"], writes=["rm"])
            em.op("act", lambda e: e.activation(rm[:, 0:TPC], rm[:, 0:TPC], AF.Exp, scale=-0.5),
                  reads=["rm"], writes=["rm"])
            em.op("dve", lambda e: e.tensor_mul(rm[:, TPC:], rm[:, 0:TPC], statv[:, 0:TPC]),
                  reads=["rm", "statv"], writes=["rm"])
            for g in range(4):
                c0 = g * 512
                for half in range(2):
                    em.op("pe", lambda e, c0=c0, half=half: e.matmul(
                        pbc[:, half * 512:(half + 1) * 512], ones1[:],
                        rm[:, half * TPC + c0: half * TPC + c0 + 512],
                        start=True, stop=True),
                        reads=["rm", "ones1"], writes=["pbc"])
                for mo in range(8):
                    em.op("dve", lambda e, mo=mo, c0=c0: e.tensor_tensor(
                        sqt[:], xsum[mo][:, c0:c0 + 512], pbc[:, 0:512], ALU.mult),
                        reads=[(f"xsum{mo}", c0), "pbc"], writes=["sqt"])
                    em.op("dve", lambda e: e.tensor_sub(st2[:], sqt[:], pbc[:, 512:1024]),
                          reads=["sqt", "pbc"], writes=["st2"])
                    em.op("sp", lambda e, mo=mo, c0=c0: e.dma_start(
                        out=o_x1[mo * 128:(mo + 1) * 128, c0:c0 + 512], in_=st2[:]),
                        reads=["st2"], dma=True)
        em.drain()
    return nc


def emit_fp4(em, nc, src, src_keys, dst, dst_key, F, scr):
    if isinstance(src_keys, str):
        src_keys = [src_keys]
    v2, tA, tB, tI, bm, r12, s12 = scr
    NB = F // 16
    em.op("dve", lambda e: e.tensor_reduce(bm[:, :NB, None], src.rearrange("p (b k) -> p b k", k=16),
                                           AX.X, ALU.max, apply_absolute_value=True),
          reads=src_keys, writes=["q_bm"])
    em.op("dve", lambda e: e.tensor_scalar_max(bm[:, :NB], bm[:, :NB], 6e-12),
          reads=["q_bm"], writes=["q_bm"])
    em.op("dve", lambda e: e.reciprocal(r12[:, :NB], bm[:, :NB]), reads=["q_bm"], writes=["q_r"])
    em.op("dve", lambda e: e.tensor_scalar_mul(r12[:, :NB], r12[:, :NB], 12.0),
          reads=["q_r"], writes=["q_r"])
    em.op("dve", lambda e: e.tensor_scalar_mul(s12[:, :NB], bm[:, :NB], 1.0 / 6.0),
          reads=["q_bm"], writes=["q_s"])
    b3 = lambda t: t[:, 0:F].rearrange("p (b k) -> p b k", k=16)
    bc3 = lambda t: t[:, :NB, None].broadcast_to([128, NB, 16])
    em.op("dve", lambda e: e.tensor_tensor(b3(v2), b3(src), bc3(r12), ALU.mult),
          reads=src_keys + ["q_r"], writes=["q_v2"])
    em.op("act", lambda e: e.activation(tA[:, 0:F], v2[:, 0:F], AF.Abs, scale=0.5),
          reads=["q_v2"], writes=["q_tA"])
    em.op("dve", lambda e: e.tensor_scalar(tB[:, 0:F], tA[:, 0:F], 5.0, 2.0, ALU.is_gt, ALU.mult),
          reads=["q_tA"], writes=["q_tB"])
    em.op("dve", lambda e: e.tensor_scalar(tA[:, 0:F], tA[:, 0:F], -2.0, 0.0, ALU.add, ALU.max),
          reads=["q_tA"], writes=["q_tA"])
    em.op("dve", lambda e: e.tensor_scalar_min(tI[:, 0:F], tA[:, 0:F], 2.0),
          reads=["q_tA"], writes=["q_tI"])
    em.op("dve", lambda e: e.tensor_copy(tA[:, 0:F], tI[:, 0:F]), reads=["q_tI"], writes=["q_tA"])
    em.op("dve", lambda e: e.tensor_add(tB[:, 0:F], tA[:, 0:F], tB[:, 0:F]),
          reads=["q_tA", "q_tB"], writes=["q_tB"])
    em.op("act", lambda e: e.activation(tA[:, 0:F], v2[:, 0:F], AF.Sign),
          reads=["q_v2"], writes=["q_tA"])
    em.op("dve", lambda e: e.tensor_mul(tB[:, 0:F], tA[:, 0:F], tB[:, 0:F]),
          reads=["q_tA", "q_tB"], writes=["q_tB"])
    em.op("dve", lambda e: e.tensor_scalar(tI[:, 0:F], v2[:, 0:F], 4.0, -4.0, ALU.min, ALU.max),
          reads=["q_v2"], writes=["q_tI"])
    em.op("dve", lambda e: e.tensor_scalar_mul(tA[:, 0:F], tI[:, 0:F], 0.5),
          reads=["q_tI"], writes=["q_tA"])
    em.op("dve", lambda e: e.tensor_add(tA[:, 0:F], tA[:, 0:F], tB[:, 0:F]),
          reads=["q_tA", "q_tB"], writes=["q_tA"])
    em.op("dve", lambda e: e.tensor_tensor(b3(dst), b3(tA), bc3(s12), ALU.mult),
          reads=["q_tA", "q_s"], writes=[dst_key])


def build_b():
    nc = bass.Bass("TRN2", target_bir_lowering=False, debug=False)
    DH = 2816
    x1tok = nc.dram_tensor("x1tok", [TPC, D_MODEL], F32, kind="ExternalInput").ap()
    wg = nc.dram_tensor("wg", [DH, D_MODEL], F32, kind="ExternalInput").ap()
    wu = nc.dram_tensor("wu", [DH, D_MODEL], F32, kind="ExternalInput").ap()
    wdn = nc.dram_tensor("wdn", [D_MODEL, DH], F32, kind="ExternalInput").ap()
    o_x2 = nc.dram_tensor("x2tok", [TPC, D_MODEL], F32, kind="ExternalOutput").ap()

    from contextlib import ExitStack
    ctx = ExitStack()
    sb = lambda name, shape, dt: ctx.enter_context(nc.sbuf_tensor(name, shape, dt))

    stage = sb("stage", [128, DH], F32)
    HH = DH // 2
    v2 = sb("v2", [128, HH], F32)
    tA = sb("tA", [128, HH], F32)
    tB = sb("tB", [128, HH], F32)
    tI = sb("tI", [128, HH], I32)
    bm = sb("bm", [128, HH // 16], F32)
    r12 = sb("r12", [128, HH // 16], F32)
    s12 = sb("s12", [128, HH // 16], F32)
    scr = (v2, tA, tB, tI, bm, r12, s12)
    idb = sb("idb", [128, 128], BF16)
    wq = sb("wq", [128, DH], BF16)
    xqT = [sb(f"xqT{k}", [128, 128], BF16) for k in range(8)]
    wgT = [sb(f"wgT{k}", [128, DH], BF16) for k in range(8)]
    wuT = [sb(f"wuT{k}", [128, DH], BF16) for k in range(8)]
    wdT = [sb(f"wdT{j}", [128, D_MODEL], BF16) for j in range(22)]
    ht = sb("ht", [128, DH], F32)
    hq = sb("hq", [128, DH], BF16)
    hqT = [sb(f"hqT{j}", [128, 128], BF16) for j in range(22)]
    xr = sb("xr", [128, D_MODEL], F32)
    mvar = sb("mvar", [128, 4], F32)
    epst = sb("epst", [1, 1], F32)
    epsb = sb("epsb", [128, 1], F32)

    with ctx, nc.Block() as block:
        em = Emitter(nc, ctx)
        em.op("pool", lambda e: e.memset(epsb[:], 1e-5), writes=["epsb"])
        em.op("pool", lambda e: e.memset(idb[:], 0.0), writes=["idb"])
        em.op("pool", lambda e: e.affine_select(out=idb[:], in_=idb[:], compare_op=ALU.not_equal,
                                                fill=1.0, base=0, pattern=[[-1, 128]],
                                                channel_multiplier=1),
              reads=["idb"], writes=["idb"])

        with ExitStack() as pctx:
            ptr = pctx.enter_context(nc.psum_tensor("ptr", [128, 128], BF16))
            for wsrc, wdstT, nm in [(wg, wgT, "wgT"), (wu, wuT, "wuT")]:
                for c in range(22):
                    em.op("sp", lambda e, c=c, wsrc=wsrc: e.dma_start(
                        out=stage[:, 0:D_MODEL], in_=wsrc[c * 128:(c + 1) * 128, :]),
                        writes=["stage"], dma=True)
                    emit_fp4(em, nc, stage[:, 0:D_MODEL], "stage", wq[:, 0:D_MODEL], "wq", D_MODEL, scr)
                    for k in range(8):
                        em.op("pe", lambda e, k=k: e.transpose(
                            ptr[:], wq[:, k * 128:(k + 1) * 128], idb[:]),
                            reads=["wq", "idb"], writes=["ptr"])
                        em.op("act", lambda e, c=c, k=k, wdstT=wdstT, nm=nm: e.copy(
                            wdstT[k][:, c * 128:(c + 1) * 128], ptr[:]),
                            reads=["ptr"], writes=[nm + str(k)])
            for c in range(8):
                em.op("sp", lambda e, c=c: e.dma_start(
                    out=stage[:], in_=wdn[c * 128:(c + 1) * 128, :]),
                    writes=["stage"], dma=True)
                emit_fp4(em, nc, stage[:, 0:HH], "stage", wq[:, 0:HH], "wq0", HH, scr)
                emit_fp4(em, nc, stage[:, HH:DH], "stage", wq[:, HH:DH], "wq", HH, scr)
                for j in range(22):
                    em.op("pe", lambda e, j=j: e.transpose(
                        ptr[:], wq[:, j * 128:(j + 1) * 128], idb[:]),
                        reads=["wq", "wq0", "idb"], writes=["ptr"])
                    em.op("act", lambda e, c=c, j=j: e.copy(
                        wdT[j][:, c * 128:(c + 1) * 128], ptr[:]),
                        reads=["ptr"], writes=[f"wdT{j}"])

        em.fence()
        with ExitStack() as pctx:
            pg = pctx.enter_context(nc.psum_tensor("pg", [128, 512], F32))
            pu = pctx.enter_context(nc.psum_tensor("pu", [128, 512], F32))
            pd = [pctx.enter_context(nc.psum_tensor(f"pd{i}", [128, 512], F32)) for i in range(2)]
            ptr2 = pctx.enter_context(nc.psum_tensor("ptr2", [128, 128], BF16))
            GF = [(i * 512, min(512, DH - i * 512)) for i in range((DH + 511) // 512)]
            for tc in range(16):
                em.op("sp", lambda e, tc=tc: e.dma_start(
                    out=stage[:, 0:D_MODEL], in_=x1tok[tc * 128:(tc + 1) * 128, :]),
                    writes=["stage"], dma=True)
                emit_fp4(em, nc, stage[:, 0:D_MODEL], "stage", wq[:, 0:D_MODEL], "wq", D_MODEL, scr)
                for k in range(8):
                    em.op("pe", lambda e, k=k: e.transpose(
                        ptr2[:], wq[:, k * 128:(k + 1) * 128], idb[:]),
                        reads=["wq", "idb"], writes=["ptr2"])
                    em.op("act", lambda e, k=k: e.copy(xqT[k][:], ptr2[:]),
                          reads=["ptr2"], writes=[f"xqT{k}"])
                for (g0, gl) in GF:
                    for k in range(8):
                        em.op("pe", lambda e, g0=g0, gl=gl, k=k: e.matmul(
                            pg[:, 0:gl], xqT[k][:], wgT[k][:, g0:g0 + gl],
                            start=(k == 0), stop=(k == 7)),
                            reads=[f"xqT{k}", f"wgT{k}"], writes=["pg"] if k in (0, 7) else [])
                        em.op("pe", lambda e, g0=g0, gl=gl, k=k: e.matmul(
                            pu[:, 0:gl], xqT[k][:], wuT[k][:, g0:g0 + gl],
                            start=(k == 0), stop=(k == 7)),
                            reads=[f"xqT{k}", f"wuT{k}"], writes=["pu"] if k in (0, 7) else [])
                    em.op("act", lambda e, g0=g0, gl=gl: e.activation(
                        ht[:, g0:g0 + gl], pg[:, 0:gl], AF.Silu),
                        reads=["pg"], writes=[("ht", g0)])
                    em.op("dve", lambda e, g0=g0, gl=gl: e.tensor_tensor(
                        ht[:, g0:g0 + gl], ht[:, g0:g0 + gl], pu[:, 0:gl], ALU.mult),
                        reads=[("ht", g0), "pu"], writes=[("ht", g0)])
                htkeys = [("ht", g0) for (g0, gl) in GF]
                emit_fp4(em, nc, ht[:, 0:HH], htkeys, hq[:, 0:HH], "hq0", HH, scr)
                emit_fp4(em, nc, ht[:, HH:DH], htkeys, hq[:, HH:DH], "hq", HH, scr)
                for j in range(22):
                    em.op("pe", lambda e, j=j: e.transpose(
                        ptr2[:], hq[:, j * 128:(j + 1) * 128], idb[:]),
                        reads=["hq", "hq0", "idb"], writes=["ptr2"])
                    em.op("act", lambda e, j=j: e.copy(hqT[j][:], ptr2[:]),
                          reads=["ptr2"], writes=[f"hqT{j}"])
                for half in range(2):
                    for j in range(22):
                        em.op("pe", lambda e, half=half, j=j: e.matmul(
                            pd[half][:], hqT[j][:], wdT[j][:, half * 512:(half + 1) * 512],
                            start=(j == 0), stop=(j == 21)),
                            reads=[f"hqT{j}", f"wdT{j}"], writes=[f"pd{half}"] if j in (0, 21) else [])
                for half in range(2):
                    em.op("dve", lambda e, half=half: e.tensor_add(
                        xr[:, half * 512:(half + 1) * 512], pd[half][:],
                        stage[:, half * 512:(half + 1) * 512]),
                        reads=[f"pd{half}", "stage"], writes=["xr"] if half == 1 else [])
                em.op("dve", lambda e: e.tensor_reduce(mvar[:, 0:1], xr[:, 0:D_MODEL], AX.X, ALU.add),
                      reads=["xr"], writes=["mv0"])
                em.op("act", lambda e: e.activation(tA[:, 0:D_MODEL], xr[:, 0:D_MODEL], AF.Square,
                                                    accum_out=mvar[:, 1:2]),
                      reads=["xr"], writes=["mv1", "q_tA"])
                em.op("dve", lambda e: e.tensor_scalar_mul(mvar[:, 0:2], mvar[:, 0:2], 1.0 / D_MODEL),
                      reads=["mv0", "mv1"], writes=["mv0", "mv1"])
                em.op("dve", lambda e: e.scalar_tensor_tensor(
                    mvar[:, 2:3], mvar[:, 0:1], -1.0, mvar[:, 0:1], ALU.mult, ALU.mult),
                    reads=["mv0"], writes=["mv2"])
                em.op("dve", lambda e: e.tensor_add(mvar[:, 2:3], mvar[:, 2:3], mvar[:, 1:2]),
                      reads=["mv1", "mv2"], writes=["mv2"])
                em.op("act", lambda e: e.activation(mvar[:, 2:3], mvar[:, 2:3], AF.Ln,
                                                    bias=epsb[:, 0:1]),
                      reads=["mv2"], scalar_reads=["epsb"], writes=["mv2"])
                em.op("act", lambda e: e.activation(mvar[:, 2:3], mvar[:, 2:3], AF.Exp, scale=-0.5),
                      reads=["mv2"], writes=["mv2"])
                em.op("dve", lambda e: e.tensor_scalar(
                    xr[:, 0:D_MODEL], xr[:, 0:D_MODEL], mvar[:, 0:1], mvar[:, 2:3],
                    ALU.subtract, ALU.mult),
                    reads=["xr"], scalar_reads=["mv0", "mv2"], writes=["xr"])
                em.op("sp", lambda e, tc=tc: e.dma_start(
                    out=o_x2[tc * 128:(tc + 1) * 128, :], in_=xr[:, 0:D_MODEL]),
                    reads=["xr"], dma=True)
        em.drain()
    return nc


def build_c():
    nc = bass.Bass("TRN2", target_bir_lowering=False, debug=False)
    W = 512
    xwT = nc.dram_tensor("xwT", [D_MODEL, 4 * W], F32, kind="ExternalInput").ap()
    wqkv = nc.dram_tensor("wqkv", [D_MODEL, 3 * D_MODEL], F32, kind="ExternalInput").ap()
    bqkv = nc.dram_tensor("bqkv", [128, 24], F32, kind="ExternalInput").ap()
    wout = nc.dram_tensor("wout", [D_MODEL, D_MODEL], F32, kind="ExternalInput").ap()
    bout = nc.dram_tensor("bout", [1, D_MODEL], F32, kind="ExternalInput").ap()
    xres = nc.dram_tensor("xres_tok", [TPC, D_MODEL], F32, kind="ExternalInput").ap()
    o_x3 = nc.dram_tensor("x3tok", [TPC, D_MODEL], F32, kind="ExternalOutput").ap()

    from contextlib import ExitStack
    ctx = ExitStack()
    sb = lambda name, shape, dt: ctx.enter_context(nc.sbuf_tensor(name, shape, dt))
    ps = lambda name, shape, dt=F32: ctx.enter_context(nc.psum_tensor(name, shape, dt))

    stage = sb("stage", [128, 3 * D_MODEL], F32)
    wqkvb = [sb(f"wqkvb{k}", [128, 3 * D_MODEL], BF16) for k in range(8)]
    woutb = [sb(f"woutb{k}", [128, D_MODEL], BF16) for k in range(8)]
    bqt = sb("bqt", [128, 24], F32)
    boutb = sb("boutb", [128, D_MODEL], F32)
    ones1 = sb("ones1", [1, 128], F32)
    brow = sb("brow", [1, D_MODEL], F32)
    idb = sb("idb", [128, 128], BF16)
    xwb = [sb(f"xwb{k}", [128, W], BF16) for k in range(8)]
    qkvt = [sb(f"qkvt{m}", [128, W], BF16) for m in range(24)]
    exps = sb("exps", [128, W], F32)
    smax = sb("smax", [128, 4], F32)
    probsT = [sb(f"probsT{kc}", [128, W], BF16) for kc in range(4)]
    probs = sb("probs", [128, W], BF16)
    vT = [sb(f"vT{kc}", [128, 64], BF16) for kc in range(4)]
    owt = [sb(f"owt{k}", [128, W], BF16) for k in range(8)]
    xr = sb("xr", [128, D_MODEL], F32)
    mvar = sb("mvar", [128, 4], F32)
    epsb = sb("epsb", [128, 1], F32)

    pq = [ps(f"pq{i}", [128, W]) for i in range(2)]
    pscore = ps("pscore", [128, W])
    ptr = ps("ptr", [128, 128], BF16)
    pouth = ps("pouth", [64, W])
    pop = [ps(f"pop{i}", [128, 512]) for i in range(2)]

    with ctx, nc.Block() as block:
        em = Emitter(nc, ctx)
        em.op("pool", lambda e: e.memset(epsb[:], 1e-5), writes=["epsb"])
        em.op("pool", lambda e: e.memset(ones1[:], 1.0), writes=["ones1"])
        em.op("pool", lambda e: e.memset(idb[:], 0.0), writes=["idb"])
        em.op("pool", lambda e: e.affine_select(out=idb[:], in_=idb[:], compare_op=ALU.not_equal,
                                                fill=1.0, base=0, pattern=[[-1, 128]],
                                                channel_multiplier=1),
              reads=["idb"], writes=["idb"])
        em.op("sp", lambda e: e.dma_start(out=bqt[:], in_=bqkv), writes=["bqt"], dma=True)
        em.op("sp", lambda e: e.dma_start(out=brow[:], in_=bout), writes=["brow"], dma=True)
        for k in range(8):
            em.op("sp", lambda e, k=k: e.dma_start(out=stage[:], in_=wqkv[k * 128:(k + 1) * 128, :]),
                  writes=["stage"], dma=True)
            em.op("dve", lambda e, k=k: e.tensor_copy(wqkvb[k][:], stage[:]),
                  reads=["stage"], writes=[f"wqkvb{k}"])
        for k in range(8):
            em.op("sp", lambda e, k=k: e.dma_start(out=stage[:, 0:D_MODEL], in_=wout[k * 128:(k + 1) * 128, :]),
                  writes=["stage"], dma=True)
            em.op("dve", lambda e, k=k: e.tensor_copy(woutb[k][:], stage[:, 0:D_MODEL]),
                  reads=["stage"], writes=[f"woutb{k}"])
        em.op("pe", lambda e: e.matmul(pop[0][:, 0:512], ones1[:], brow[:, 0:512],
                                       start=True, stop=True),
              reads=["ones1", "brow"], writes=["pop0"])
        em.op("act", lambda e: e.copy(boutb[:, 0:512], pop[0][:, 0:512]), reads=["pop0"], writes=["boutb"])
        em.op("pe", lambda e: e.matmul(pop[0][:, 0:512], ones1[:], brow[:, 512:1024],
                                       start=True, stop=True),
              reads=["ones1", "brow"], writes=["pop0"])
        em.op("act", lambda e: e.copy(boutb[:, 512:1024], pop[0][:, 0:512]), reads=["pop0"], writes=["boutb"])

        for w in range(4):
            for k in range(8):
                em.op("sp", lambda e, w=w, k=k: e.dma_start(
                    out=stage[:, 0:W], in_=xwT[k * 128:(k + 1) * 128, w * W:(w + 1) * W]),
                    writes=["stage"], dma=True)
                em.op("dve", lambda e, k=k: e.tensor_copy(xwb[k][:], stage[:, 0:W]),
                      reads=["stage"], writes=[f"xwb{k}"])
            for m in range(24):
                pt = pq[m % 2]
                pkey = f"pq{m % 2}"
                for k in range(8):
                    em.op("pe", lambda e, m=m, k=k, pt=pt: e.matmul(
                        pt[:], wqkvb[k][:, m * 128:(m + 1) * 128], xwb[k][:],
                        start=(k == 0), stop=(k == 7)),
                        reads=[f"wqkvb{k}", f"xwb{k}"], writes=[pkey] if k in (0, 7) else [])
                em.op("act", lambda e, m=m, pt=pt: e.activation(
                    qkvt[m][:], pt[:], AF.Identity, bias=bqt[:, m:m + 1]),
                    reads=[pkey], scalar_reads=["bqt"], writes=[f"qkvt{m}"])
            for h in range(16):
                qc_t = qkvt[h // 2]
                kc_t = qkvt[8 + h // 2]
                vc_t = qkvt[16 + h // 2]
                off = (h % 2) * 64
                qkey, kkey, vkey = f"qkvt{h // 2}", f"qkvt{8 + h // 2}", f"qkvt{16 + h // 2}"
                for kc in range(4):
                    em.op("pe", lambda e, kc=kc, vc_t=vc_t, off=off: e.transpose(
                        ptr[:, 0:64], vc_t[off:off + 64, kc * 128:(kc + 1) * 128], idb[off:off + 64, off:off + 64]),
                        reads=[vkey, "idb"], writes=["ptr"])
                    em.op("act", lambda e, kc=kc: e.copy(vT[kc][:], ptr[:, 0:64]),
                          reads=["ptr"], writes=[f"vT{kc}"])
                for qc in range(4):
                    em.op("pe", lambda e, qc=qc, qc_t=qc_t, kc_t=kc_t, off=off: e.matmul(
                        pscore[:], qc_t[off:off + 64, qc * 128:(qc + 1) * 128], kc_t[off:off + 64, :],
                        start=True, stop=True),
                        reads=[qkey, kkey], writes=["pscore"])
                    em.op("dve", lambda e, qc=qc: e.tensor_reduce(
                        smax[:, 0:1], pscore[:], AX.X, ALU.max),
                        reads=["pscore"], writes=["smax"])
                    em.op("dve", lambda e: e.tensor_scalar_mul(smax[:, 1:2], smax[:, 0:1], -0.125),
                          reads=["smax"], writes=["smax1"])
                    em.op("act", lambda e: e.activation(
                        exps[:], pscore[:], AF.Exp, bias=smax[:, 1:2], scale=0.125,
                        accum_out=smax[:, 2:3]),
                        reads=["pscore"], scalar_reads=["smax1"], writes=["exps", "smax2"])
                    em.op("dve", lambda e: e.reciprocal(smax[:, 3:4], smax[:, 2:3]),
                          reads=["smax2"], writes=["smax3"])
                    em.op("dve", lambda e: e.tensor_scalar(
                        probs[:], exps[:], smax[:, 3:4], None, ALU.mult),
                        reads=["exps"], scalar_reads=["smax3"], writes=["probs"])
                    for kc in range(4):
                        em.op("pe", lambda e, kc=kc: e.transpose(
                            ptr[:], probs[:, kc * 128:(kc + 1) * 128], idb[:]),
                            reads=["probs", "idb"], writes=["ptr"])
                        em.op("act", lambda e, kc=kc, qc=qc: e.copy(
                            probsT[kc][:, qc * 128:(qc + 1) * 128], ptr[:]),
                            reads=["ptr"], writes=[f"probsT{kc}"])
                for kc in range(4):
                    em.op("pe", lambda e, kc=kc: e.matmul(
                        pouth[:], vT[kc][:], probsT[kc][:],
                        start=(kc == 0), stop=(kc == 3)),
                        reads=[f"vT{kc}", f"probsT{kc}"], writes=["pouth"] if kc in (0, 3) else [])
                em.op("act", lambda e, h=h: e.copy(owt[h // 2][(h % 2) * 64:(h % 2) * 64 + 64, :], pouth[:]),
                      reads=["pouth"], writes=[(f"owt{h // 2}", h % 2)])
            for ts4 in range(4):
                for half in range(2):
                    for kc in range(8):
                        em.op("pe", lambda e, ts4=ts4, half=half, kc=kc: e.matmul(
                            pop[half][:], owt[kc][:, ts4 * 128:(ts4 + 1) * 128],
                            woutb[kc][:, half * 512:(half + 1) * 512],
                            start=(kc == 0), stop=(kc == 7)),
                            reads=[(f"owt{kc}", 0), (f"owt{kc}", 1), f"woutb{kc}"],
                            writes=[f"pop{half}"] if kc in (0, 7) else [])
                trow = w * 512 + ts4 * 128
                em.op("sp", lambda e, trow=trow: e.dma_start(
                    out=stage[:, 0:D_MODEL], in_=xres[trow:trow + 128, :]),
                    writes=["stage"], dma=True)
                for half in range(2):
                    em.op("dve", lambda e, half=half: e.tensor_add(
                        xr[:, half * 512:(half + 1) * 512], pop[half][:],
                        stage[:, half * 512:(half + 1) * 512]),
                        reads=[f"pop{half}", "stage"], writes=[("xr", half)])
                em.op("dve", lambda e: e.tensor_add(xr[:], xr[:], boutb[:]),
                      reads=[("xr", 0), ("xr", 1), "boutb"], writes=["xr"])
                em.op("dve", lambda e: e.tensor_reduce(mvar[:, 0:1], xr[:], AX.X, ALU.add),
                      reads=["xr"], writes=["mv0"])
                em.op("act", lambda e: e.activation(exps[:, 0:512], xr[:, 0:512], AF.Square,
                                                    accum_out=mvar[:, 1:2]),
                      reads=["xr"], writes=["mv1", "exps"])
                em.op("act", lambda e: e.activation(exps[:, 0:512], xr[:, 512:1024], AF.Square,
                                                    accum_out=mvar[:, 2:3]),
                      reads=["xr"], writes=["mv2", "exps"])
                em.op("dve", lambda e: e.tensor_add(mvar[:, 1:2], mvar[:, 1:2], mvar[:, 2:3]),
                      reads=["mv1", "mv2"], writes=["mv1"])
                em.op("dve", lambda e: e.tensor_scalar_mul(mvar[:, 0:2], mvar[:, 0:2], 1.0 / D_MODEL),
                      reads=["mv0", "mv1"], writes=["mv0", "mv1"])
                em.op("dve", lambda e: e.scalar_tensor_tensor(
                    mvar[:, 2:3], mvar[:, 0:1], -1.0, mvar[:, 0:1], ALU.mult, ALU.mult),
                    reads=["mv0"], writes=["mv2"])
                em.op("dve", lambda e: e.tensor_add(mvar[:, 2:3], mvar[:, 2:3], mvar[:, 1:2]),
                      reads=["mv1", "mv2"], writes=["mv2"])
                em.op("act", lambda e: e.activation(mvar[:, 2:3], mvar[:, 2:3], AF.Ln, bias=epsb[:, 0:1]),
                      reads=["mv2"], scalar_reads=["epsb"], writes=["mv2"])
                em.op("act", lambda e: e.activation(mvar[:, 2:3], mvar[:, 2:3], AF.Exp, scale=-0.5),
                      reads=["mv2"], writes=["mv2"])
                em.op("dve", lambda e: e.tensor_scalar(
                    xr[:], xr[:], mvar[:, 0:1], mvar[:, 2:3], ALU.subtract, ALU.mult),
                    reads=["xr"], scalar_reads=["mv0", "mv2"], writes=["xr"])
                em.op("sp", lambda e, trow=trow: e.dma_start(
                    out=o_x3[trow:trow + 128, :], in_=xr[:]),
                    reads=["xr"], dma=True)
        em.drain()
    return nc


def build_d():
    nc = bass.Bass("TRN2", target_bir_lowering=False, debug=False)
    DH = 4096
    TH = TPC + 2
    x3T = nc.dram_tensor("x3T", [D_MODEL, TH], F32, kind="ExternalInput").ap()
    w1T = nc.dram_tensor("w1T", [D_MODEL, DH], F32, kind="ExternalInput").ap()
    cw = nc.dram_tensor("cw", [128, 32 * 3], F32, kind="ExternalInput").ap()
    cb = nc.dram_tensor("cb", [128, 32], F32, kind="ExternalInput").ap()
    w2T = nc.dram_tensor("w2T", [DH, D_MODEL], F32, kind="ExternalInput").ap()
    xres = nc.dram_tensor("x3res_tok", [TPC, D_MODEL], F32, kind="ExternalInput").ap()
    o_out = nc.dram_tensor("out_tok", [TPC, D_MODEL], F32, kind="ExternalOutput").ap()

    from contextlib import ExitStack
    ctx = ExitStack()
    sb = lambda name, shape, dt: ctx.enter_context(nc.sbuf_tensor(name, shape, dt))

    TPAD = 2080
    NT32 = TPAD // 32
    w1s = [sb(f"w1s{k}", [128, DH], BF16) for k in range(8)]
    w2s = [sb(f"w2s{k}", [128, D_MODEL], BF16) for k in range(32)]
    stage = sb("stage", [128, 2052], F32)
    scr4 = sb("scr4", [128, 2048], F32)
    scaleb = sb("scaleb", [128, TPAD], BF16)
    invb = sb("invb", [128, TPAD], BF16)
    m32t = sb("m32t", [32, TPAD], BF16)
    tr32 = sb("tr32", [32, 1056], BF16)
    cmt = sb("cmt", [32, 2 * NT32], F32)
    cmb = sb("cmb", [32, 144], BF16)
    selq = sb("selq", [32, 32 * 128], BF16)
    xqi = [sb(f"xqi{k}", [128, 258], BF16) for k in range(8)]
    xld = sb("xld", [128, 258], F32)
    acc1 = sb("acc1", [128, 2], F32)
    sc01 = sb("sc01", [1, 4], F32)
    m1b = sb("m1b", [128, 2], F32)
    hmul = sb("hmul", [128, 2], F32)
    invT = sb("invT", [128, 2], F32)
    onesc = sb("onesc", [128, 1], F32)
    ones1 = sb("ones1", [1, 128], F32)
    cwt = sb("cwt", [128, 96], F32)
    cbt = sb("cbt", [128, 32], F32)
    ht = sb("ht", [128, 258], F32)
    h2q = [sb(f"h2q{m}", [128, 256], BF16) for m in range(32)]
    habs = sb("habs", [128, 256], F32)
    xr = sb("xr", [128, D_MODEL], F32)
    mvar = sb("mvar", [128, 4], F32)
    epsb = sb("epsb", [128, 1], F32)

    with ctx, nc.Block() as block:
        em = Emitter(nc, ctx)
        em.op("pool", lambda e: e.memset(epsb[:], 1e-5), writes=["epsb"])
        em.op("pool", lambda e: e.memset(onesc[:], 1.0), writes=["onesc"])
        em.op("pool", lambda e: e.memset(ones1[:], 1.0), writes=["ones1"])
        em.op("sp", lambda e: e.dma_start(out=cwt[:], in_=cw), writes=["cwt"], dma=True)
        em.op("sp", lambda e: e.dma_start(out=cbt[:], in_=cb), writes=["cbt"], dma=True)
        em.op("pool", lambda e: e.memset(selq[:], 1.0), writes=["selq"])
        for n in range(32):
            em.op("pool", lambda e, n=n: e.affine_select(
                out=selq[:, n * 128:(n + 1) * 128], in_=selq[:, n * 128:(n + 1) * 128],
                compare_op=ALU.is_equal, fill=0.0, base=-n, pattern=[[0, 128]],
                channel_multiplier=1), reads=["selq"], writes=["selq"] if n == 31 else [])

        with ExitStack() as pctx:
            pp = pctx.enter_context(nc.psum_tensor("pp", [128, 512], F32))
            pq = pctx.enter_context(nc.psum_tensor("pq", [1, 512], F32))
            for wsrc, nchunk, wcols, acol in [(w1T, 8, DH, 0), (w2T, 32, D_MODEL, 1)]:
                first = True
                for c in range(nchunk):
                    for h0 in range(0, wcols, 2048):
                        cl = min(2048, wcols - h0)
                        em.op("sp", lambda e, c=c, h0=h0, cl=cl, wsrc=wsrc: e.dma_start(
                            out=stage[:, 0:cl], in_=wsrc[c * 128:(c + 1) * 128, h0:h0 + cl]),
                            writes=["stage"], dma=True)
                        if first:
                            em.op("dve", lambda e, acol=acol, cl=cl: e.tensor_reduce(
                                acc1[:, acol:acol + 1], stage[:, 0:cl], AX.X, ALU.add,
                                apply_absolute_value=True),
                                reads=["stage"], writes=[f"acc{acol}"])
                            first = False
                        else:
                            em.op("dve", lambda e, cl=cl: e.tensor_reduce(
                                mvar[:, 0:1], stage[:, 0:cl], AX.X, ALU.add,
                                apply_absolute_value=True),
                                reads=["stage"], writes=["mtmp"])
                            em.op("dve", lambda e, acol=acol: e.tensor_add(
                                acc1[:, acol:acol + 1], acc1[:, acol:acol + 1], mvar[:, 0:1]),
                                reads=["mtmp", f"acc{acol}"], writes=[f"acc{acol}"])
            em.op("pe", lambda e: e.matmul(pq[:, 0:2], onesc[:], acc1[:, 0:2], start=True, stop=True),
                  reads=["acc0", "acc1", "onesc"], writes=["pq"])
            em.op("act", lambda e: e.copy(sc01[:, 0:2], pq[:, 0:2]), reads=["pq"], writes=["sc01"])
            em.op("dve", lambda e: e.tensor_scalar_mul(sc01[:, 0:2], sc01[:, 0:2], 1.0 / (D_MODEL * DH)),
                  reads=["sc01"], writes=["sc01"])
            em.op("dve", lambda e: e.tensor_scalar_max(sc01[:, 0:2], sc01[:, 0:2], 1e-5),
                  reads=["sc01"], writes=["sc01"])
            em.op("pe", lambda e: e.matmul(pp[:, 0:2], ones1[:], sc01[:, 0:2], start=True, stop=True),
                  reads=["sc01", "ones1"], writes=["pp"])
            em.op("act", lambda e: e.copy(m1b[:, 0:2], pp[:, 0:2]), reads=["pp"], writes=["m1b"])
            em.op("dve", lambda e: e.tensor_scalar_mul(hmul[:, 0:2], m1b[:, 0:2], 0.5),
                  reads=["m1b"], writes=["hmul"])

            for wsrc, dst, nchunk, wcols, acol, nm in [(w1T, w1s, 8, DH, 0, "w1s"),
                                                       (w2T, w2s, 32, D_MODEL, 1, "w2s")]:
                for c in range(nchunk):
                    for h0 in range(0, wcols, 2048):
                        cl = min(2048, wcols - h0)
                        em.op("sp", lambda e, c=c, h0=h0, cl=cl, wsrc=wsrc: e.dma_start(
                            out=stage[:, 0:cl], in_=wsrc[c * 128:(c + 1) * 128, h0:h0 + cl]),
                            writes=["stage"], dma=True)
                        em.op("act", lambda e, cl=cl: e.activation(scr4[:, 0:cl], stage[:, 0:cl], AF.Abs),
                              reads=["stage"], writes=["scr4"])
                        em.op("dve", lambda e, cl=cl, acol=acol: e.tensor_scalar(
                            scr4[:, 0:cl], scr4[:, 0:cl], hmul[:, acol:acol + 1], None, ALU.is_gt),
                            reads=["scr4"], scalar_reads=["hmul"], writes=["scr4"])
                        em.op("act", lambda e, cl=cl: e.activation(invb[:, 0:cl], stage[:, 0:cl], AF.Sign),
                              reads=["stage"], writes=["invb"])
                        em.op("dve", lambda e, c=c, h0=h0, cl=cl, dst=dst, nm=nm: e.tensor_tensor(
                            dst[c][:, h0:h0 + cl], scr4[:, 0:cl], invb[:, 0:cl], ALU.mult),
                            reads=["scr4", "invb"], writes=[nm + str(c)])

            for k in range(8):
                em.op("sp", lambda e, k=k: e.dma_start(
                    out=stage[:, 0:1025], in_=x3T[k * 128:(k + 1) * 128, 0:1025]),
                    writes=["stage"], dma=True)
                em.op("sp", lambda e, k=k: e.dma_start(
                    out=stage[:, 1025:2050], in_=x3T[k * 128:(k + 1) * 128, 1025:2050]),
                    writes=["stage"], dma=True)
                if k == 0:
                    em.op("pool", lambda e: e.memset(scaleb[:], 0.0), writes=["scaleb"])
                    em.op("act", lambda e: e.activation(scaleb[:, 0:TH], stage[:, 0:TH], AF.Abs),
                          reads=["stage"], writes=["scaleb"])
                else:
                    em.op("act", lambda e: e.activation(invb[:, 0:TH], stage[:, 0:TH], AF.Abs),
                          reads=["stage"], writes=["invb"])
                    em.op("dve", lambda e: e.tensor_max(scaleb[:, 0:TH], scaleb[:, 0:TH], invb[:, 0:TH]),
                          reads=["scaleb", "invb"], writes=["scaleb"])
            em.op("pool", lambda e: e.memset(m32t[:], 0.0), writes=["m32t"])
            em.op("dve", lambda e: e.tensor_copy(invb[0:64, :], scaleb[64:128, :]),
                  reads=["scaleb"], writes=["invb"])
            em.op("dve", lambda e: e.tensor_max(scaleb[0:64, :], scaleb[0:64, :], invb[0:64, :]),
                  reads=["scaleb", "invb"], writes=["scaleb"])
            em.op("dve", lambda e: e.tensor_copy(invb[0:32, :], scaleb[32:64, :]),
                  reads=["scaleb"], writes=["invb"])
            em.op("dve", lambda e: e.tensor_max(m32t[:, :], scaleb[0:32, :], invb[0:32, :]),
                  reads=["scaleb", "invb", "m32t"], writes=["m32t"])
            for (h0, hl) in [(0, 1024), (1024, 1056)]:
                em.op("dve", lambda e, h0=h0, hl=hl: e.transpose(tr32[:, 0:hl], m32t[:, h0:h0 + hl]),
                      reads=["m32t"], writes=["tr32"])
                em.op("dve", lambda e, h0=h0, hl=hl: e.tensor_reduce(
                    cmt[:, h0 // 32:(h0 + hl) // 32],
                    tr32[:, 0:hl].rearrange("p (k q) -> p k q", q=32), AX.X, ALU.max),
                    reads=["tr32"], writes=["cmt"])
            em.op("dve", lambda e: e.tensor_scalar_max(cmt[:, 0:NT32], cmt[:, 0:NT32], 1e-5),
                  reads=["cmt"], writes=["cmt"])
            em.op("dve", lambda e: e.reciprocal(cmt[:, NT32:2 * NT32], cmt[:, 0:NT32]),
                  reads=["cmt"], writes=["cmt2"])
            em.op("dve", lambda e: e.tensor_scalar_mul(cmt[:, NT32:2 * NT32], cmt[:, NT32:2 * NT32], 127.0),
                  reads=["cmt2"], writes=["cmt2"])
            em.op("dve", lambda e: e.tensor_scalar_mul(cmt[:, 0:NT32], cmt[:, 0:NT32], 1.0 / 127.0),
                  reads=["cmt"], writes=["cmt"])
            em.op("dve", lambda e: e.tensor_copy(cmb[:, 0:2 * NT32], cmt[:, 0:2 * NT32]),
                  reads=["cmt", "cmt2"], writes=["cmb"])
            for n in range(32):
                em.op("pe", lambda e, n=n: e.matmul(
                    pp[:, 0:2 * NT32], selq[:, n * 128:(n + 1) * 128], cmb[:, 0:2 * NT32],
                    start=True, stop=True),
                    reads=["selq", "cmb"], writes=["pp"])
                em.op("act", lambda e, n=n: e.copy(
                    scaleb.rearrange("p (k q) -> p k q", q=32)[:, :, n], pp[:, NT32:2 * NT32]),
                    reads=["pp"], writes=["scaleb"])
                em.op("act", lambda e, n=n: e.activation(
                    invb.rearrange("p (k q) -> p k q", q=32)[:, :, n], pp[:, 0:NT32],
                    AF.Identity, scale=m1b[:, 0:1]),
                    reads=["pp"], scalar_reads=["m1b"], writes=["invb"])

        em.fence()
        with ExitStack() as pctx:
            ph = pctx.enter_context(nc.psum_tensor("ph", [128, 258], F32))
            po = [pctx.enter_context(nc.psum_tensor(f"po{i}", [128, 512], F32)) for i in range(2)]
            pcm = pctx.enter_context(nc.psum_tensor("pcm", [128, 16], F32))
            for ts8 in range(8):
                t0 = ts8 * 256
                for k in range(8):
                    em.op("sp", lambda e, k=k, t0=t0: e.dma_start(
                        out=xld[:], in_=x3T[k * 128:(k + 1) * 128, t0:t0 + 258]),
                        writes=["xld"], dma=True)
                    em.op("dve", lambda e, t0=t0: e.tensor_tensor(
                        scr4.bitcast(I32)[:, 0:258], xld[:], scaleb[:, t0:t0 + 258], ALU.mult),
                        reads=["xld", "scaleb"], writes=["scr4"])
                    em.op("dve", lambda e, k=k: e.tensor_copy(xqi[k][:], scr4.bitcast(I32)[:, 0:258]),
                          reads=["scr4"], writes=[f"xqi{k}"])
                for m in range(32):
                    for k in range(8):
                        em.op("pe", lambda e, m=m, k=k: e.matmul(
                            ph[:], w1s[k][:, m * 128:(m + 1) * 128], xqi[k][:],
                            start=(k == 0), stop=(k == 7)),
                            reads=[f"w1s{k}", f"xqi{k}"], writes=["ph"] if k in (0, 7) else [])
                    em.op("dve", lambda e, m=m, t0=t0: e.tensor_tensor(
                        ht[:], ph[:], invb[:, t0:t0 + 258], ALU.mult),
                        reads=["ph", "invb"], writes=["ht"])
                    em.op("dve", lambda e, m=m: e.tensor_scalar(
                        stage[:, 0:256], ht[:, 0:256], cwt[:, m * 3:m * 3 + 1], None, ALU.mult),
                        reads=["ht"], scalar_reads=["cwt"], writes=["stage"])
                    for kk in range(1, 3):
                        em.op("dve", lambda e, m=m, kk=kk: e.scalar_tensor_tensor(
                            stage[:, 0:256], ht[:, kk:kk + 256], cwt[:, m * 3 + kk:m * 3 + kk + 1],
                            stage[:, 0:256], ALU.mult, ALU.add),
                            reads=["ht", "stage"], scalar_reads=["cwt"], writes=["stage"])
                    em.op("act", lambda e, m=m: e.activation(
                        stage[:, 0:256], stage[:, 0:256], AF.Silu, bias=cbt[:, m:m + 1]),
                        reads=["stage"], scalar_reads=["cbt"], writes=["stage"])
                    em.op("dve", lambda e, m=m: e.tensor_copy(h2q[m][:], stage[:, 0:256]),
                          reads=["stage"], writes=[f"h2q{m}"])
                    if m == 0:
                        em.op("act", lambda e: e.activation(scr4[:, 0:256], stage[:, 0:256], AF.Abs),
                              reads=["stage"], writes=["scr4"])
                    else:
                        em.op("act", lambda e: e.activation(habs[:], stage[:, 0:256], AF.Abs),
                              reads=["stage"], writes=["habs"])
                        em.op("dve", lambda e: e.tensor_max(scr4[:, 0:256], scr4[:, 0:256], habs[:]),
                              reads=["scr4", "habs"], writes=["scr4"])
                em.op("dve", lambda e: e.tensor_copy(habs[0:64, :], scr4[64:128, 0:256]),
                      reads=["scr4"], writes=["habs"])
                em.op("dve", lambda e: e.tensor_max(scr4[0:64, 0:256], scr4[0:64, 0:256], habs[0:64, :]),
                      reads=["scr4", "habs"], writes=["scr4"])
                em.op("dve", lambda e: e.tensor_copy(habs[0:32, :], scr4[32:64, 0:256]),
                      reads=["scr4"], writes=["habs"])
                em.op("dve", lambda e: e.tensor_max(m32t[:, 0:256], scr4[0:32, 0:256], habs[0:32, :]),
                      reads=["scr4", "habs", "m32t"], writes=["m32t"])
                em.op("dve", lambda e: e.transpose(tr32[:, 0:256], m32t[:, 0:256]),
                      reads=["m32t"], writes=["tr32"])
                em.op("dve", lambda e: e.tensor_reduce(
                    cmt[:, 0:8], tr32[:, 0:256].rearrange("p (k q) -> p k q", q=32), AX.X, ALU.max),
                    reads=["tr32"], writes=["cmt"])
                em.op("dve", lambda e: e.tensor_scalar_max(cmt[:, 0:8], cmt[:, 0:8], 1e-5),
                      reads=["cmt"], writes=["cmt"])
                em.op("dve", lambda e: e.reciprocal(cmt[:, 8:16], cmt[:, 0:8]),
                      reads=["cmt"], writes=["cmt2"])
                em.op("dve", lambda e: e.tensor_scalar_mul(cmt[:, 8:16], cmt[:, 8:16], 127.0),
                      reads=["cmt2"], writes=["cmt2"])
                em.op("dve", lambda e: e.tensor_scalar_mul(cmt[:, 0:8], cmt[:, 0:8], 1.0 / 127.0),
                      reads=["cmt"], writes=["cmt"])
                em.op("dve", lambda e: e.tensor_copy(cmb[:, 0:16], cmt[:, 0:16]),
                      reads=["cmt", "cmt2"], writes=["cmb"])
                for n in range(32):
                    em.op("pe", lambda e, n=n: e.matmul(
                        pcm[:, 0:16], selq[:, n * 128:(n + 1) * 128], cmb[:, 0:16],
                        start=True, stop=True),
                        reads=["selq", "cmb"], writes=["pcm"])
                    em.op("act", lambda e, n=n: e.copy(
                        habs[:].rearrange("p (k q) -> p k q", q=32)[:, :, n], pcm[:, 8:16]),
                        reads=["pcm"], writes=["habs"])
                for tt in range(2):
                    for k4 in range(4):
                        em.op("pool", lambda e, tt=tt, k4=k4: e.dma_start(
                            out=invT[32 * k4:32 * k4 + 32, tt:tt + 1],
                            in_=cmt[:, 4 * tt + k4:4 * tt + k4 + 1]),
                            reads=["cmt"], writes=["invTr"], dma=True)
                em.op("dve", lambda e: e.tensor_scalar(
                    invT[:, 0:2], invT[:, 0:2], m1b[:, 1:2], None, ALU.mult),
                    reads=["invTr"], scalar_reads=["m1b"], writes=["invT"])
                for m in range(32):
                    em.op("dve", lambda e, m=m: e.tensor_tensor(
                        scr4.bitcast(I32)[:, 0:256], h2q[m][:], habs[:], ALU.mult),
                        reads=[f"h2q{m}", "habs"], writes=["scr4"])
                    em.op("dve", lambda e, m=m: e.tensor_copy(h2q[m][:], scr4.bitcast(I32)[:, 0:256]),
                          reads=["scr4"], writes=[f"h2q{m}"])
                for tt in range(2):
                    for half in range(2):
                        for k in range(32):
                            em.op("pe", lambda e, tt=tt, half=half, k=k: e.matmul(
                                po[half][:], h2q[k][:, tt * 128:(tt + 1) * 128],
                                w2s[k][:, half * 512:(half + 1) * 512],
                                start=(k == 0), stop=(k == 31)),
                                reads=[f"h2q{k}", f"w2s{k}"],
                                writes=[f"po{half}"] if k in (0, 31) else [])
                    trow = t0 + tt * 128
                    em.op("sp", lambda e, trow=trow: e.dma_start(
                        out=stage[:, 0:D_MODEL], in_=xres[trow:trow + 128, :]),
                        writes=["stage"], dma=True)
                    for half in range(2):
                        em.op("dve", lambda e, tt=tt, half=half: e.tensor_scalar(
                            xr[:, half * 512:(half + 1) * 512], po[half][:],
                            invT[:, tt:tt + 1], None, ALU.mult),
                            reads=[f"po{half}"], scalar_reads=["invT"],
                            writes=[("xr", half)])
                    for half in range(2):
                        em.op("dve", lambda e, half=half: e.tensor_add(
                            xr[:, half * 512:(half + 1) * 512], xr[:, half * 512:(half + 1) * 512],
                            stage[:, half * 512:(half + 1) * 512]),
                            reads=[("xr", half), "stage"], writes=[("xr", half)])
                    em.op("dve", lambda e: e.tensor_reduce(mvar[:, 0:1], xr[:], AX.X, ALU.add),
                          reads=[("xr", 0), ("xr", 1)], writes=["mv0"])
                    em.op("act", lambda e: e.activation(scr4[:, 0:1024], xr[:], AF.Square,
                                                        accum_out=mvar[:, 1:2]),
                          reads=[("xr", 0), ("xr", 1)], writes=["mv1", "scr4"])
                    em.op("dve", lambda e: e.tensor_scalar_mul(mvar[:, 0:2], mvar[:, 0:2], 1.0 / D_MODEL),
                          reads=["mv0", "mv1"], writes=["mv0", "mv1"])
                    em.op("dve", lambda e: e.scalar_tensor_tensor(
                        mvar[:, 2:3], mvar[:, 0:1], -1.0, mvar[:, 0:1], ALU.mult, ALU.mult),
                        reads=["mv0"], writes=["mv2"])
                    em.op("dve", lambda e: e.tensor_add(mvar[:, 2:3], mvar[:, 2:3], mvar[:, 1:2]),
                          reads=["mv1", "mv2"], writes=["mv2"])
                    em.op("act", lambda e: e.activation(mvar[:, 2:3], mvar[:, 2:3], AF.Ln, bias=epsb[:, 0:1]),
                          reads=["mv2"], scalar_reads=["epsb"], writes=["mv2"])
                    em.op("act", lambda e: e.activation(mvar[:, 2:3], mvar[:, 2:3], AF.Exp, scale=-0.5),
                          reads=["mv2"], writes=["mv2"])
                    em.op("dve", lambda e: e.tensor_scalar(
                        xr[:], xr[:], mvar[:, 0:1], mvar[:, 2:3], ALU.subtract, ALU.mult),
                        reads=[("xr", 0), ("xr", 1)], scalar_reads=["mv0", "mv2"],
                        writes=[("xr", 0), ("xr", 1)])
                    em.op("sp", lambda e, trow=trow: e.dma_start(
                        out=o_out[trow:trow + 128, :], in_=xr[:]),
                        reads=[("xr", 0), ("xr", 1)], dma=True)
        em.drain()
    return nc


_BUILD_CACHE = {}


def _get(name, builder):
    if name not in _BUILD_CACHE:
        _BUILD_CACHE[name] = builder()
    return _BUILD_CACHE[name]


def kernel(**inputs):
    inputs = {k: np.asarray(v) for k, v in inputs.items()}
    cores = list(range(N_CORES))

    r1 = run_bass_kernel_spmd(_get("a1", build_a1), _host_shards_a1(inputs), core_ids=cores).results

    def asm_f(key):
        out = np.empty((D_INNER, BATCH, SEQ), dtype=np.asarray(r1[0][key]).dtype)
        for c in range(N_CORES):
            b, h = divmod(c, 2)
            out[:, b, h * TPC:(h + 1) * TPC] = r1[c][key]
        return out

    dt_f = asm_f("dt")
    dtx_f = asm_f("dtx")
    bc_f = np.empty((BATCH, 32, SEQ), np.float32)
    for c in range(N_CORES):
        b, h = divmod(c, 2)
        bc_f[b, :, h * TPC:(h + 1) * TPC] = r1[c]["bc"]
    A_log = inputs["L0_attn_A_log"]
    a2_maps = []
    for c in range(N_CORES):
        dsl = slice(256 * c, 256 * (c + 1))
        al = A_log[dsl].reshape(2, 128, 16).transpose(1, 0, 2).reshape(128, 32)
        a2_maps.append({
            "dtp": np.ascontiguousarray(dt_f[dsl]),
            "dtxp": np.ascontiguousarray(dtx_f[dsl]),
            "bcp": bc_f,
            "alog": np.ascontiguousarray(al),
        })
    r2 = run_bass_kernel_spmd(_get("a2", build_a2), a2_maps, core_ids=cores).results
    y_f = np.empty((D_INNER, BATCH, SEQ), np.float32)
    for c in range(N_CORES):
        y_f[256 * c:256 * (c + 1)] = r2[c]["y"]

    x_in = inputs["L0_attn_x"]
    WoT = np.ascontiguousarray(inputs["L0_attn_W_out"].T)
    Dv = np.ascontiguousarray(inputs["L0_attn_D"].reshape(16, 128).T)
    a3_maps = []
    for c in range(N_CORES):
        b, h = divmod(c, 2)
        sl = slice(h * TPC, (h + 1) * TPC)
        a3_maps.append({
            "yT": np.ascontiguousarray(y_f[:, b, sl]),
            "zsT": r1[c]["zs"],
            "xcT": r1[c]["xconv"],
            "Dv": Dv, "WoT": WoT,
            "xresT": np.ascontiguousarray(x_in[b, sl, :].T),
        })
    r3 = run_bass_kernel_spmd(_get("a3", build_a3), a3_maps, core_ids=cores).results

    b_maps = []
    for c in range(N_CORES):
        b_maps.append({
            "x1tok": np.ascontiguousarray(np.asarray(r3[c]["x1"]).T),
            "wg": inputs["L0_ffn_w_gate"], "wu": inputs["L0_ffn_w_up"],
            "wdn": inputs["L0_ffn_w_down"],
        })
    rb = run_bass_kernel_spmd(_get("b", build_b), b_maps, core_ids=cores).results
    x2_full = np.empty((BATCH, SEQ, D_MODEL), np.float32)
    for c in range(N_CORES):
        b, h = divmod(c, 2)
        x2_full[b, h * TPC:(h + 1) * TPC] = rb[c]["x2tok"]

    bq = np.ascontiguousarray(inputs["L1_attn_b_qkv"].reshape(24, 128).T)
    bo = np.ascontiguousarray(inputs["L1_attn_b_out"].reshape(1, 1024))
    c_maps = []
    for c in range(N_CORES):
        b, h = divmod(c, 2)
        wins = range(4 * h, 4 * h + 4)
        xw = np.concatenate([x2_full[b, 256 * n:256 * n + 512, :].T for n in wins], axis=1)
        c_maps.append({
            "xwT": np.ascontiguousarray(xw),
            "wqkv": inputs["L1_attn_w_qkv"], "bqkv": bq,
            "wout": inputs["L1_attn_w_out"], "bout": bo,
            "xres_tok": np.ascontiguousarray(x2_full[b, h * TPC:(h + 1) * TPC, :]),
        })
    rc = run_bass_kernel_spmd(_get("c", build_c), c_maps, core_ids=cores).results
    x3_full = np.empty((BATCH, SEQ, D_MODEL), np.float32)
    for c in range(N_CORES):
        b, h = divmod(c, 2)
        x3_full[b, h * TPC:(h + 1) * TPC] = rc[c]["x3tok"]

    w1T = np.ascontiguousarray(inputs["L1_ffn_w1"].T)
    w2T = np.ascontiguousarray(inputs["L1_ffn_w2"].T)
    cwm = np.ascontiguousarray(inputs["L1_ffn_conv_w"].reshape(4096, 3)
                               .reshape(32, 128, 3).transpose(1, 0, 2).reshape(128, 96))
    cbm = np.ascontiguousarray(inputs["L1_ffn_conv_b"].reshape(32, 128).T)
    d_maps = []
    for c in range(N_CORES):
        b, h = divmod(c, 2)
        t0 = h * TPC
        xh = np.zeros((D_MODEL, TPC + 2), np.float32)
        lo = max(0, t0 - 1); hi = min(SEQ, t0 + TPC + 1)
        slv = x3_full[b, lo:hi, :].T
        off = 1 - (t0 - lo)
        xh[:, off:off + slv.shape[1]] = slv
        d_maps.append({
            "x3T": xh, "w1T": w1T, "cw": cwm, "cb": cbm, "w2T": w2T,
            "x3res_tok": np.ascontiguousarray(x3_full[b, t0:t0 + TPC, :]),
        })
    rd = run_bass_kernel_spmd(_get("d", build_d), d_maps, core_ids=cores).results
    out = np.empty((BATCH, SEQ, D_MODEL), np.float32)
    for c in range(N_CORES):
        b, h = divmod(c, 2)
        out[b, h * TPC:(h + 1) * TPC] = rd[c]["out_tok"]
    return out
